# version 01 (final)
# baseline (speedup 1.0000x reference)
import math
import numpy as np
import ml_dtypes
from contextlib import ExitStack

import concourse.bass as bass
import concourse.mybir as mybir
from concourse.alu_op_type import AluOpType as Op
from concourse.bass_utils import run_bass_kernel_spmd

F32 = mybir.dt.float32
BF16 = mybir.dt.bfloat16
AF = mybir.ActivationFunctionType
AX = mybir.AxisListType.X

T = 2048
TH = T + 3
THP = 2064
H = 2048
IN = 4096
NH = 64
PH = 64
Q = 128
NCH = T // Q
PROJ = 8512
MAGIC = 12582912.0
BIGF = 3.0e38
KH = H // 128
KI = IN // 128
EPS = 1e-5


class SemC:
    def __init__(self, stack, nc, name):
        self.h = stack.enter_context(nc.semaphore(name))
        self.n = 0

    def inc(self, instr, k=1):
        instr.then_inc(self.h, k)
        self.n += k
        return instr

    def inc16(self, instr):
        return self.inc(instr, 16)

    def wait(self, eng, n=None):
        eng.wait_ge(self.h, self.n if n is None else n)


class Builder:
    def __init__(self, taps=()):
        self.taps = set(taps)
        nc = self.nc = bass.Bass("TRN2", target_bir_lowering=False)
        self.st = ExitStack()
        e = self.e = self.st.enter_context

        D = nc.dram_tensor
        self.xh = D("xh", [TH, H], F32, kind="ExternalInput").ap()
        self.w_in = D("w_in", [PROJ, H], F32, kind="ExternalInput").ap()
        self.w_out = D("w_out", [H, IN], F32, kind="ExternalInput").ap()
        self.w_up = D("w_up", [IN, H], F32, kind="ExternalInput").ap()
        self.w_down = D("w_down", [H, IN], F32, kind="ExternalInput").ap()
        self.cwcols = D("cwcols", [128, 136], F32, kind="ExternalInput").ap()
        self.cbcols = D("cbcols", [128, 34], F32, kind="ExternalInput").ap()
        self.dtb_col = D("dtb_col", [64, 1], F32, kind="ExternalInput").ap()
        self.alog_col = D("alog_col", [64, 1], F32, kind="ExternalInput").ap()
        self.d_rep = D("d_rep", [128, 64], F32, kind="ExternalInput").ap()
        self.normw_rep = D("normw_rep", [128, IN], F32, kind="ExternalInput").ap()
        self.selm_rep = D("selm_rep", [64, 16], F32, kind="ExternalInput").ap()
        self.sel_rep = D("sel_rep", [64, 4], F32, kind="ExternalInput").ap()

        self.out = D("out", [T, H], F32, kind="ExternalOutput").ap()

        self.xbh = D("xbh", [THP, H], BF16, kind="Internal").ap()
        self.bW_in = D("bW_in", [PROJ, H], BF16, kind="Internal").ap()
        self.bW_out = D("bW_out", [H, IN], BF16, kind="Internal").ap()
        self.tW_up = D("tW_up", [IN, H], BF16, kind="Internal").ap()
        self.tW_down = D("tW_down", [H, IN], BF16, kind="Internal").ap()
        self.xbc_st = D("xbc_st", [34, 128, TH], BF16, kind="Internal").ap()
        self.xcbh = D("xcbh", [4352, T], BF16, kind="Internal").ap()
        self.z_st = D("z_st", [T, IN], BF16, kind="Internal").ap()
        self.yp_st = D("yp_st", [T, IN], BF16, kind="Internal").ap()
        self.yfbh = D("yfbh", [T, IN], BF16, kind="Internal").ap()
        self.attn_st = D("attn_st", [H, T], BF16, kind="Internal").ap()
        self.q2bh = D("q2bh", [T, IN], BF16, kind="Internal").ap()
        self.cc_in = D("cc_in", [128, IN + 4], F32, kind="Internal").ap()
        self.cc_out = D("cc_out", [4 * 128, IN + 4], F32, kind="Internal").ap()
        self.idhbm = D("idhbm", [128, 128], BF16, kind="Internal").ap()

        self.tap_t = {}
        for name, shape, dt in [
            ("xbc", [34, 128, TH], BF16), ("z", [T, IN], BF16), ("dt", [64, T], F32),
            ("xc", [4352, T], BF16), ("yp", [T, IN], BF16),
            ("sloc", [128, IN + 4], F32), ("eg", [128, NCH * 64], F32),
            ("yf", [T, IN], BF16), ("attn", [H, T], BF16), ("x1", [T, H], F32),
            ("sin", [128, IN], F32), ("tup", [IN, H], BF16), ("tdn", [H, IN], BF16),
            ("q2", [T, IN], BF16),
        ]:
            if name in self.taps:
                self.tap_t[name] = D("tap_" + name, shape, dt, kind="ExternalOutput").ap()

        S = lambda *a: e(nc.sbuf_tensor(*a))
        self.ident = S("ident", [128, 128], BF16)
        self.ident_f = S("ident_f", [128, 128], F32)
        self.lt01 = S("lt01", [128, 128], F32)
        self.ones_f1 = S("ones_f1", [1, 128], F32)
        self.onecol_bf = S("onecol_bf", [128, 1], BF16)
        self.onecol_f = S("onecol_f", [128, 1], F32)
        self.zero128 = S("zero128", [128, 128], BF16)
        self.cw = S("cw", [128, 136], F32)
        self.cb = S("cb", [128, 34], F32)
        self.dtb = S("dtb", [64, 1], F32)
        self.negA = S("negA", [64, 1], F32)
        self.dcols = S("dcols", [128, 64], F32)
        self.selmt = S("selmt", [64, 16], F32)
        self.selt = S("selt", [64, 4], F32)
        self.offc = S("offc", [64, 1], F32)
        self.supc = S("supc", [128, 1], F32)
        self.sdnc = S("sdnc", [128, 1], F32)
        self.rsup = S("rsup", [128, 1], F32)
        self.rsdn = S("rsdn", [128, 1], F32)
        self.c1c = S("c1c", [128, 1], F32)
        self.epscol = S("epscol", [128, 1], F32)
        self.ln127c = S("ln127c", [128, 1], F32)
        self.inv2c = S("inv2c", [128, NCH], F32)
        self.mid_scope = ExitStack()
        me = self.mid_scope.enter_context
        self.dt_fm = me(nc.sbuf_tensor("dt_fm", [64, T], F32))
        self.eglobT = me(nc.sbuf_tensor("eglobT", [128, NCH * 64], F32))
        self.S_in_bf = me(nc.sbuf_tensor("S_in_bf", [128, IN], BF16))

        self.psA = e(nc.psum_tensor("psA", [128, 2048], F32))
        self.psY = e(nc.psum_tensor("psY", [128, 1024], F32))
        self.psS = e(nc.psum_tensor("psS", [128, 512], F32))
        self.psM = e(nc.psum_tensor("psM", [128, 512], F32))

        self.dS = SemC(self.st, nc, "dS")
        self.dC = SemC(self.st, nc, "dC")
        self.dG = SemC(self.st, nc, "dG")
        self.vS = SemC(self.st, nc, "vS")
        self.sS = SemC(self.st, nc, "sS")
        self.tS = SemC(self.st, nc, "tS")
        self.gS = SemC(self.st, nc, "gS")
        self.ccS = SemC(self.st, nc, "ccS")
        self.all_sems = [self.dS, self.dC, self.dG, self.vS, self.sS,
                         self.tS, self.gS, self.ccS]

        for s in self.all_sems:
            nc.gpsimd.sem_clear(s.h)

        self.block = e(nc.Block())

    def bank(self, i):
        if i < 4:
            return self.psA[:, i * 512:(i + 1) * 512]
        if i < 6:
            return self.psY[:, (i - 4) * 512:(i - 3) * 512]
        return (self.psS if i == 6 else self.psM)[:]

    def fence(self):
        b = self.block
        me = {"vector": self.vS, "scalar": self.sS, "tensor": self.tS,
              "gpsimd": self.gS}

        @b.vector
        def _(e): self.vS.inc(e.drain())
        @b.scalar
        def _(e): self.sS.inc(e.drain())
        @b.tensor
        def _(e): self.tS.inc(e.drain())
        @b.gpsimd
        def _(e): self.gS.inc(e.drain())

        for name, deco in [("vector", b.vector), ("scalar", b.scalar),
                           ("tensor", b.tensor), ("gpsimd", b.gpsimd),
                           ("sync", b.sync)]:
            @deco
            def _(e, name=name):
                for s in (self.vS, self.sS, self.tS, self.gS):
                    if me.get(name) is not s:
                        s.wait(e)
                self.dS.wait(e)
                self.dC.wait(e)
                self.dG.wait(e)

    def tap_dump(self, name, src_dram):
        if name not in self.tap_t:
            return
        @self.block.sync
        def _(s):
            self.dS.inc16(s.dma_start(out=self.tap_t[name], in_=src_dram))

    def tap_dump_sbuf(self, name, src_ap):
        if name not in self.tap_t:
            return
        @self.block.sync
        def _(s):
            self.dS.inc16(s.dma_start(out=self.tap_t[name], in_=src_ap))

    def pro(self):
        b = self.block
        nc = self.nc

        @b.gpsimd
        def _(g):
            self.dG.inc16(g.dma_start(out=self.xbh[0:TH, :], in_=self.xh))
            for i in range(4):
                self.dG.inc16(g.dma_start(out=self.bW_in[i * 2128:(i + 1) * 2128, :],
                                          in_=self.w_in[i * 2128:(i + 1) * 2128, :]))
            for i in range(2):
                self.dG.inc16(g.dma_start(out=self.bW_out[i * 1024:(i + 1) * 1024, :],
                                          in_=self.w_out[i * 1024:(i + 1) * 1024, :]))
        self.xbh_done = 16
        self.win_done = 16 * 5
        self.wout_done = 16 * 7

        @b.vector
        def _(v):
            v.memset(self.zero128[:], 0.0)
            v.memset(self.onecol_bf[:], 1.0)
            v.memset(self.onecol_f[:], 1.0)
            v.memset(self.ones_f1[:], 1.0)
            v.memset(self.lt01[:], 0.0)
            v.memset(self.epscol[:], EPS)
            v.memset(self.ln127c[:], math.log(127.0))
            for l in range(128):
                v.memset(self.lt01[0:l + 1, l:l + 1], 1.0)
            self.vS.inc(v.memset(self.offc[:], 0.0))
        consts_v = self.vS.n

        @b.sync
        def _(s):
            for dst, src in [(self.cw, self.cwcols), (self.cb, self.cbcols),
                             (self.dtb, self.dtb_col), (self.dcols, self.d_rep),
                             (self.selmt, self.selm_rep), (self.selt, self.sel_rep)]:
                self.dS.inc16(s.dma_start(out=dst[:], in_=src))
            self.dS.inc16(s.dma_start(out=self.negA[:], in_=self.alog_col))
            s.wait_ge(self.vS.h, consts_v)
            self.dS.inc16(s.dma_start(out=self.idhbm, in_=self.zero128[:]))
            self.dS.wait(s)
            diag = self.idhbm.rearrange("a b -> (a b)")[0:128 * 128:129].unsqueeze(1)
            self.dS.inc16(s.dma_start(out=diag, in_=self.onecol_bf[:],
                                      allow_slow_non_contiguous=True))
            self.dS.wait(s)
            self.dS.inc16(s.dma_start(out=self.ident[:], in_=self.idhbm))

        @b.scalar
        def _(s):
            self.dS.wait(s)
            s.activation(out=self.negA[:], in_=self.negA[:], func=AF.Exp)
            s.mul(out=self.negA[:], in_=self.negA[:], mul=-1.0)
            self.sS.inc(s.copy(out=self.ident_f[:], in_=self.ident[:]))

        self.a_scope = ExitStack()
        ae = self.a_scope.enter_context
        self.x_fm = ae(nc.sbuf_tensor("x_fm", [128, KH * THP], BF16))

        @b.scalar
        def _(s):
            s.wait_ge(self.dG.h, self.xbh_done)
            for k in range(KH):
                self.dC.inc16(s.dma_start(
                    out=self.x_fm[:, k * THP:k * THP + 2048],
                    in_=self.xbh[0:2048, k * 128:(k + 1) * 128], transpose=True))
                self.dC.inc16(s.dma_start(
                    out=self.x_fm[:, k * THP + 2048:k * THP + THP],
                    in_=self.xbh[2048:THP, k * 128:(k + 1) * 128], transpose=True))
        self.fence()

    def a1(self, slabs=None, ngroups=5):
        b = self.block
        nc = self.nc
        ae = self.a_scope.enter_context
        wsa = [ae(nc.sbuf_tensor(f"wsa{i}", [128, KH * 1024], BF16)) for i in range(2)]
        stg = [ae(nc.sbuf_tensor(f"stgA{i}", [128, TH], BF16)) for i in range(2)]

        if slabs is None:
            slabs = [(4096, 1024), (5120, 1024), (6144, 1024), (7168, 1024), (8192, 320)]
        groups = [(0, 3), (3, 512), (515, 512), (1027, 512), (1539, 512)][:ngroups]

        units = []
        slab_last_unit = {}
        for si, (r0, sz) in enumerate(slabs):
            for oci in range((sz + 127) // 128):
                o0 = oci * 128
                ow = min(128, sz - o0)
                is_dt = (r0 + o0 >= 8448)
                for (t0, tw) in (groups[1:] if is_dt else groups):
                    units.append((si, o0, ow, t0, tw, is_dt, len(units) % 8))
            slab_last_unit[si] = len(units)

        dC0 = self.dC.n
        tS0 = self.tS.n
        vS0 = self.vS.n
        dS0 = self.dS.n

        oc_of_unit = []
        for (si, o0, ow, t0, tw, is_dt, bk) in units:
            oc_of_unit.append(None if is_dt else (slabs[si][0] + o0 - 4096) // 128)
        last_unit_of_oc = {}
        for ui, oc in enumerate(oc_of_unit):
            if oc is not None:
                last_unit_of_oc[oc] = ui

        @b.scalar
        def _(s):
            s.wait_ge(self.dG.h, self.win_done)
            for si, (r0, sz) in enumerate(slabs):
                if si >= 2:
                    s.wait_ge(self.tS.h, tS0 + slab_last_unit[si - 2])
                for k in range(KH):
                    self.dC.inc16(s.dma_start(
                        out=wsa[si % 2][:, k * 1024:k * 1024 + sz],
                        in_=self.bW_in[r0:r0 + sz, k * 128:(k + 1) * 128],
                        transpose=True))

        @b.tensor
        def _(t):
            for ui, (si, o0, ow, t0, tw, is_dt, bk) in enumerate(units):
                t.wait_ge(self.dC.h, dC0 + 16 * KH * (si + 1))
                if ui >= 8:
                    t.wait_ge(self.vS.h, vS0 + ui - 7)
                for k in range(KH):
                    mm = t.matmul(self.bank(bk)[0:ow, 0:tw],
                                  wsa[si % 2][:, k * 1024 + o0:k * 1024 + o0 + ow],
                                  self.x_fm[:, k * THP + t0:k * THP + t0 + tw],
                                  start=(k == 0), stop=(k == KH - 1))
                self.tS.inc(mm)

        @b.vector
        def _(v):
            for ui, (si, o0, ow, t0, tw, is_dt, bk) in enumerate(units):
                v.wait_ge(self.tS.h, tS0 + ui + 1)
                if is_dt:
                    self.vS.inc(v.tensor_copy(self.dt_fm[:, t0 - 3:t0 - 3 + tw],
                                              self.bank(bk)[0:64, 0:tw]))
                else:
                    oc = oc_of_unit[ui]
                    if oc >= 2 and t0 == 0:
                        v.wait_ge(self.dS.h, dS0 + 16 * (oc - 1))
                    self.vS.inc(v.tensor_copy(stg[oc % 2][:, t0:t0 + tw],
                                              self.bank(bk)[:, 0:tw]))

        @b.sync
        def _(s):
            for oc in sorted(last_unit_of_oc):
                s.wait_ge(self.vS.h, vS0 + last_unit_of_oc[oc] + 1)
                self.dS.inc16(s.dma_start(
                    out=self.xbc_st.rearrange("a p t -> (a p) t")[oc * 128:(oc + 1) * 128, :],
                    in_=stg[oc % 2][:, :]))

        self.fence()
        self.tap_dump("xbc", self.xbc_st)
        self.tap_dump_sbuf("dt", self.dt_fm[:])

    def a2(self):
        b = self.block
        nc = self.nc
        ae = self.a_scope.enter_context
        wz = [ae(nc.sbuf_tensor(f"wz{i}", [128, KH * 512], BF16)) for i in range(2)]
        stg = [ae(nc.sbuf_tensor(f"stgZ{i}", [128, 512], BF16)) for i in range(2)]

        dC0 = self.dC.n
        tS0 = self.tS.n
        vS0 = self.vS.n
        dS0 = self.dS.n

        @b.scalar
        def _(s):
            for g in range(8):
                if g >= 2:
                    s.wait_ge(self.tS.h, tS0 + (g - 1) * NCH)
                for k in range(KH):
                    self.dC.inc16(s.dma_start(
                        out=wz[g % 2][:, k * 512:(k + 1) * 512],
                        in_=self.bW_in[g * 512:(g + 1) * 512, k * 128:(k + 1) * 128],
                        transpose=True))

        @b.tensor
        def _(t):
            for g in range(8):
                t.wait_ge(self.dC.h, dC0 + 16 * KH * (g + 1))
                for c in range(NCH):
                    u = g * NCH + c
                    if u >= 8:
                        t.wait_ge(self.vS.h, vS0 + u - 7)
                    for k in range(KH):
                        mm = t.matmul(self.bank(u % 8),
                                      self.x_fm[:, k * THP + 3 + c * 128:k * THP + 3 + (c + 1) * 128],
                                      wz[g % 2][:, k * 512:(k + 1) * 512],
                                      start=(k == 0), stop=(k == KH - 1))
                    self.tS.inc(mm)

        @b.vector
        def _(v):
            for u in range(8 * NCH):
                v.wait_ge(self.tS.h, tS0 + u + 1)
                if u >= 2:
                    v.wait_ge(self.dS.h, dS0 + 16 * (u - 1))
                self.vS.inc(v.tensor_copy(stg[u % 2][:, :], self.bank(u % 8)))

        @b.sync
        def _(s):
            for g in range(8):
                for c in range(NCH):
                    u = g * NCH + c
                    s.wait_ge(self.vS.h, vS0 + u + 1)
                    self.dS.inc16(s.dma_start(
                        out=self.z_st[c * 128:(c + 1) * 128, g * 512:(g + 1) * 512],
                        in_=stg[u % 2][:, :]))

        self.fence()
        self.tap_dump("z", self.z_st)
        self.a_scope.close()

    def b_phase(self, nchunks=NCH):
        b = self.block
        nc = self.nc
        self.b_scope = ExitStack()
        ae = self.b_scope.enter_context
        S = lambda *a: ae(nc.sbuf_tensor(*a))
        cwdiag = S("cwdiag", [128, 136 * 128], BF16)
        S_run = self.S_run = S("S_run", [128, IN], F32)
        S_run_bf = S("S_run_bf", [128, IN], BF16)
        xcin = [S(f"xcin{i}", [128, 515], BF16) for i in range(2)]
        xcstg = [S(f"xcstg{i}", [128, 512], BF16) for i in range(2)]
        xT = S("xT", [128, IN], BF16)
        xD = S("xD", [128, IN], F32)
        xs_T = S("xs_T", [128, IN], BF16)
        xsd = S("xsd", [128, IN], BF16)
        yp_c = S("yp_c", [128, IN], F32)
        Eb = [S(f"Eb{i}", [128, 1024], BF16) for i in range(2)]
        Sw = [S(f"Sw{i}", [128, 1024], BF16) for i in range(2)]
        t1b = S("t1b", [128, 512], F32)
        t2b = S("t2b", [128, 512], F32)
        spb1 = S("spb1", [64, 128], F32)
        dtc = S("dtc", [64, 128], F32)
        ahb = S("ahb", [64, 128], F32)
        dtT = S("dtT", [128, 64], F32)
        AhT = S("AhT", [128, 64], F32)
        AcsT = S("AcsT", [128, 64], F32)
        nAcsT = S("nAcsT", [128, 64], F32)
        Acsf = S("Acsf", [64, 128], F32)
        elocT = S("elocT", [128, 64], F32)
        decf = S("decf", [64, 128], F32)
        decT = S("decT", [128, 64], F32)
        egf = S("egf", [64, 128], F32)
        cdr = S("cdr", [1, 64], F32)
        cdrep = S("cdrep", [128, 64], F32)
        acs_flat = S("acs_flat", [1, NH * Q], F32)
        B_fm = S("B_fm", [128, 128], BF16)
        B_T = S("B_T", [128, 128], BF16)
        C_fm = S("C_fm", [128, 128], BF16)
        G_lt = S("G_lt", [128, 128], BF16)

        xcf = self.xcbh.rearrange("a b -> a b")
        xbcf = self.xbc_st.rearrange("a p t -> (a p) t")

        @b.vector
        def _(v):
            for j in range(136):
                v.tensor_scalar(out=cwdiag[:, j * 128:(j + 1) * 128],
                                in0=self.ident_f[:], scalar1=self.cw[:, j:j + 1],
                                scalar2=None, op0=Op.mult)
            v.memset(S_run[:], 0.0)
            self.vS.inc(v.memset(S_run_bf[:], 0.0))

        conv_end_dG = {}
        yp_store_dG = {}
        prev = {}

        for c in range(nchunks):
            g = c // 4
            if c % 4 == 0:
                cd0 = self.dS.n
                ct0 = self.tS.n
                cs0 = self.sS.n
                cg0 = self.dG.n
                vpre = self.vS.n
                spre = self.sS.n

                @b.sync
                def _(s, g=g, ct0=ct0):
                    for pt in range(34):
                        if pt >= 2:
                            s.wait_ge(self.tS.h, ct0 + pt - 1)
                        self.dS.inc16(s.dma_start(
                            out=xcin[pt % 2][:, 0:515],
                            in_=xbcf[pt * 128:(pt + 1) * 128, g * 512:g * 512 + 515]))

                @b.tensor
                def _(t, g=g, cd0=cd0, cs0=cs0, vpre=vpre):
                    for pt in range(34):
                        t.wait_ge(self.dS.h, cd0 + 16 * (pt + 1))
                        if pt < 2:
                            t.wait_ge(self.vS.h, vpre)
                        else:
                            t.wait_ge(self.sS.h, cs0 + pt - 1)
                        bk = self.psS if pt % 2 == 0 else self.psM
                        for k in range(4):
                            mm = t.matmul(bk[:, 0:512],
                                          cwdiag[:, (4 * pt + k) * 128:(4 * pt + k + 1) * 128],
                                          xcin[pt % 2][:, k:k + 512],
                                          start=(k == 0), stop=(k == 3))
                        self.tS.inc(mm)

                @b.scalar
                def _(s, ct0=ct0, cg0=cg0, pt_n=34):
                    for pt in range(pt_n):
                        s.wait_ge(self.tS.h, ct0 + pt + 1)
                        if pt >= 2:
                            s.wait_ge(self.dG.h, cg0 + 16 * (pt - 1))
                        bk = self.psS if pt % 2 == 0 else self.psM
                        self.sS.inc(s.activation(
                            out=xcstg[pt % 2][:, 0:512], in_=bk[:, 0:512],
                            func=AF.Silu, bias=self.cb[:, pt:pt + 1], scale=1.0))

                @b.gpsimd
                def _(gp, g=g, cs0=cs0):
                    for pt in range(34):
                        gp.wait_ge(self.sS.h, cs0 + pt + 1)
                        self.dG.inc16(gp.dma_start(
                            out=xcf[pt * 128:(pt + 1) * 128, g * 512:(g + 1) * 512],
                            in_=xcstg[pt % 2][:, 0:512]))
                conv_end_dG[g] = self.dG.n

            dS_bc = self.dS.n

            @b.sync
            def _(s, c=c, g=g):
                s.wait_ge(self.dG.h, conv_end_dG[g])
                if c > 0:
                    s.wait_ge(self.tS.h, prev["Ylast"])
                self.dS.inc16(s.dma_start(
                    out=B_fm[:], in_=xcf[4096:4224, c * 128:(c + 1) * 128]))
                self.dS.inc16(s.dma_start(
                    out=C_fm[:], in_=xcf[4224:4352, c * 128:(c + 1) * 128]))
            dS_bc_end = self.dS.n

            vbase = self.vS.n
            sbase = self.sS.n
            tbase = self.tS.n
            dtr = self.dt_fm[:, c * 128:(c + 1) * 128]

            @b.scalar
            def _(s, c=c, g=g, vbase=vbase):
                if c > 0:
                    s.wait_ge(self.vS.h, prev["xsd"])
                s.wait_ge(self.dG.h, conv_end_dG[g])
                self.dC.inc16(s.dma_start(out=xT[:], in_=xcf[0:4096, c * 128:(c + 1) * 128],
                                          transpose=True))
                if c > 0:
                    s.wait_ge(self.tS.h, prev["Ylast"])
                self.dC.inc16(s.dma_start(out=B_T[:], in_=xcf[4096:4224, c * 128:(c + 1) * 128],
                                          transpose=True))
                s.activation(out=spb1[:], in_=dtr, func=AF.Abs, bias=self.dtb[:])
                s.activation(out=spb1[:], in_=spb1[:], func=AF.Exp, scale=-1.0)
                self.sS.inc(s.activation(out=spb1[:], in_=spb1[:], func=AF.Ln, bias=1.0), 3)
                s.wait_ge(self.vS.h, vbase + 9)
                s.mul(out=nAcsT[:], in_=AcsT[:], mul=-1.0)
                self.sS.inc(s.activation(out=elocT[:], in_=AcsT[:], func=AF.Exp), 2)
                s.wait_ge(self.vS.h, vbase + 10)
                self.sS.inc(s.activation(out=decf[:], in_=Acsf[:], func=AF.Exp,
                                         scale=-1.0, bias=Acsf[:, 127:128]))
                self.sS.inc(s.activation(out=egf[:], in_=Acsf[:], func=AF.Exp,
                                         bias=self.offc[:]))
                s.wait_ge(self.vS.h, vbase + 11)
                self.sS.inc(s.activation(out=cdr[:], in_=cdr[:], func=AF.Exp))

                self.dC.inc16(s.dma_start(
                    out=acs_flat[0:1, :].rearrange("a (b d) -> a b d", b=64),
                    in_=Acsf[:]))
            dC_flat = self.dC.n
            s_t3 = sbase + 3
            s_dneg = sbase + 5
            s_dec = sbase + 6
            s_eg = sbase + 7
            s_cdr = sbase + 8

            @b.vector
            def _(v, c=c, vbase=vbase, tbase=tbase, sbase=sbase):
                v.wait_ge(self.dC.h, dC_flat - 16)
                self.vS.inc(v.tensor_tensor(
                    out=xD[:].rearrange("p (a d) -> p a d", a=64),
                    in0=xT[:].rearrange("p (a d) -> p a d", a=64),
                    in1=self.dcols[:].unsqueeze(2).broadcast_to([128, 64, 64]),
                    op=Op.mult))
                self.vS.inc(v.tensor_scalar(out=dtc[:], in0=dtr, scalar1=self.dtb[:],
                                            scalar2=0.0, op0=Op.add, op1=Op.max))
                v.wait_ge(self.sS.h, s_t3)
                self.vS.inc(v.tensor_tensor(out=dtc[:], in0=dtc[:], in1=spb1[:],
                                            op=Op.add))
                self.vS.inc(v.tensor_scalar(out=dtc[:], in0=dtc[:], scalar1=1e10,
                                            scalar2=None, op0=Op.min))
                self.vS.inc(v.tensor_scalar(out=ahb[:], in0=dtc[:], scalar1=self.negA[:],
                                            scalar2=None, op0=Op.mult))
                v.wait_ge(self.tS.h, tbase + 1)
                self.vS.inc(v.tensor_copy(dtT[:], self.psM[:, 0:64]))
                self.vS.inc(v.tensor_tensor(
                    out=xs_T[:].rearrange("p (a d) -> p a d", a=64),
                    in0=xT[:].rearrange("p (a d) -> p a d", a=64),
                    in1=dtT[:].unsqueeze(2).broadcast_to([128, 64, 64]),
                    op=Op.mult))
                v.wait_ge(self.tS.h, tbase + 2)
                self.vS.inc(v.tensor_copy(AhT[:], self.psM[:, 64:128]))
                v.wait_ge(self.tS.h, tbase + 3)
                self.vS.inc(v.tensor_copy(AcsT[:], self.psM[:, 128:192]))
                v.wait_ge(self.tS.h, tbase + 4)
                self.vS.inc(v.tensor_copy(Acsf[:], self.psM[0:64, 192:320]))
                v.wait_ge(self.tS.h, tbase + 5)
                self.vS.inc(v.tensor_copy(cdr[:], self.psM[0:1, 320:384]))
                v.wait_ge(self.tS.h, tbase + 6)
                self.vS.inc(v.tensor_copy(decT[:], self.psM[:, 384:448]))
                self.vS.inc(v.tensor_tensor(
                    out=xsd[:].rearrange("p (a d) -> p a d", a=64),
                    in0=xs_T[:].rearrange("p (a d) -> p a d", a=64),
                    in1=decT[:].unsqueeze(2).broadcast_to([128, 64, 64]),
                    op=Op.mult))
                v.wait_ge(self.tS.h, tbase + 7)
                self.vS.inc(v.tensor_copy(self.eglobT[:, c * 64:(c + 1) * 64],
                                          self.psM[:, 448:512]))
                v.wait_ge(self.tS.h, tbase + 8)
                self.vS.inc(v.tensor_copy(cdrep[:], self.psY[:, 128:192]))
                v.wait_ge(self.tS.h, tbase + 9)
                self.vS.inc(v.tensor_tensor(out=G_lt[:], in0=self.psY[:, 0:128],
                                            in1=self.lt01[:], op=Op.mult))
                v.wait_ge(self.sS.h, s_eg)
                self.vS.inc(v.tensor_tensor(out=self.offc[:], in0=self.offc[:],
                                            in1=Acsf[:, 127:128], op=Op.add))
                self.vS.inc(v.tensor_tensor(
                    out=S_run[:].rearrange("p (a d) -> p a d", a=64),
                    in0=S_run[:].rearrange("p (a d) -> p a d", a=64),
                    in1=cdrep[:].unsqueeze(2).broadcast_to([128, 64, 64]),
                    op=Op.mult))
            v_xsd = vbase + 13
            vwbase = vbase + 18

            @b.tensor
            def _(t, vbase=vbase, sbase=sbase):
                t.wait_ge(self.vS.h, vbase + 4)
                t.wait_ge(self.sS.h, sbase)
                self.tS.inc(t.transpose(self.psM[:, 0:64], dtc[:], self.ident_f[0:64, 0:64]))
                t.wait_ge(self.vS.h, vbase + 6)
                self.tS.inc(t.transpose(self.psM[:, 64:128], ahb[:], self.ident_f[0:64, 0:64]))
                t.wait_ge(self.vS.h, vbase + 8)
                self.tS.inc(t.matmul(self.psM[:, 128:192], self.lt01[:], AhT[:],
                                     start=True, stop=True))
                t.wait_ge(self.vS.h, vbase + 9)
                self.tS.inc(t.transpose(self.psM[0:64, 192:320], AcsT[:], self.ident_f[:]))
                t.wait_ge(self.vS.h, vbase + 10)
                self.tS.inc(t.matmul(self.psM[0:1, 320:384], Acsf[:, 127:128],
                                     self.ident_f[0:64, 0:64], start=True, stop=True))
                t.wait_ge(self.sS.h, s_dec)
                t.wait_ge(self.vS.h, vbase + 11)
                self.tS.inc(t.transpose(self.psM[:, 384:448], decf[:], self.ident_f[0:64, 0:64]))
                t.wait_ge(self.sS.h, s_eg)
                t.wait_ge(self.vS.h, vbase + 12)
                self.tS.inc(t.transpose(self.psM[:, 448:512], egf[:], self.ident_f[0:64, 0:64]))
                t.wait_ge(self.sS.h, s_cdr)
                t.wait_ge(self.vS.h, vbase + 14)
                self.tS.inc(t.matmul(self.psY[:, 128:192], self.ones_f1[:], cdr[:],
                                     start=True, stop=True))
                t.wait_ge(self.dS.h, dS_bc_end)
                t.wait_ge(self.vS.h, vbase + 15)
                self.tS.inc(t.matmul(self.psY[:, 0:128], B_fm[:], C_fm[:],
                                     start=True, stop=True))
            twbase = tbase + 9
            swbase_pre = self.sS.n

            def tS_M(w):
                return twbase + (w + 1 if w < 2 else 2 * w)
            def tS_Y(w):
                return twbase + (2 * w + 3 if w < 7 else 16)

            @b.tensor
            def _(t, c=c):
                emitted = []
                def emit_M(w):
                    t.wait_ge(self.dC.h, dC_flat)
                    if w >= 2:
                        t.wait_ge(self.sS.h, swbase_pre + w - 1)
                    else:
                        t.wait_ge(self.sS.h, swbase_pre)
                    for j in range(2):
                        mm = t.matmul(self.psA[:, (w % 2) * 1024 + j * 512:
                                               (w % 2) * 1024 + (j + 1) * 512],
                                      self.ones_f1[:],
                                      acs_flat[0:1, w * 1024 + j * 512:w * 1024 + (j + 1) * 512],
                                      start=True, stop=True)
                    self.tS.inc(mm)
                def emit_Y(w):
                    t.wait_ge(self.vS.h, vwbase + 5 * w + 1)
                    if c > 0 and w == 0:
                        pass
                    for j in range(8):
                        h = 8 * w + j
                        t.matmul(self.psS[:, j * 64:(j + 1) * 64], C_fm[:],
                                 S_run_bf[:, h * 64:(h + 1) * 64], start=True, stop=True)
                    for j in range(8):
                        h = 8 * w + j
                        t.matmul(self.psY[:, (w % 2) * 512 + j * 64:(w % 2) * 512 + (j + 1) * 64],
                                 Sw[w % 2][:, j * 128:(j + 1) * 128],
                                 xs_T[:, h * 64:(h + 1) * 64], start=True, stop=True)
                    for j in range(8):
                        h = 8 * w + j
                        mm = t.matmul(self.psM[:, j * 64:(j + 1) * 64], B_T[:],
                                      xsd[:, h * 64:(h + 1) * 64], start=True, stop=True)
                    self.tS.inc(mm)
                emit_M(0)
                emit_M(1)
                for w in range(8):
                    emit_Y(w)
                    if w + 2 < 8:
                        emit_M(w + 2)

            @b.scalar
            def _(s, c=c, vbase=vbase):
                for w in range(8):
                    s.wait_ge(self.tS.h, tS_M(w))
                    if w >= 2:
                        s.wait_ge(self.vS.h, vwbase + 5 * (w - 2) + 1)
                    elif c > 0:
                        s.wait_ge(self.vS.h, vbase)
                    for j in range(8):
                        h = 8 * w + j
                        act = s.activation(
                            out=Eb[w % 2][:, j * 128:(j + 1) * 128],
                            in_=self.psA[:, (w % 2) * 1024 + j * 128:(w % 2) * 1024 + (j + 1) * 128],
                            func=AF.Exp, bias=nAcsT[:, h:h + 1])
                    self.sS.inc(act)

            @b.vector
            def _(v, c=c):
                for w in range(8):
                    v.wait_ge(self.sS.h, swbase_pre + w + 1)
                    if w >= 2:
                        v.wait_ge(self.tS.h, tS_Y(w - 2))
                    elif c > 0:
                        v.wait_ge(self.tS.h, prev["Ylast"])
                    self.vS.inc(v.scalar_tensor_tensor(
                        out=Sw[w % 2][:].rearrange("p (a d) -> p a d", a=8),
                        in0=Eb[w % 2][:].rearrange("p (a d) -> p a d", a=8),
                        scalar=BIGF,
                        in1=G_lt[:].unsqueeze(1).broadcast_to([128, 8, 128]),
                        op0=Op.min, op1=Op.mult))
                    v.wait_ge(self.tS.h, tS_Y(w))
                    self.vS.inc(v.tensor_tensor(
                        out=t1b[:].rearrange("p (a d) -> p a d", a=8),
                        in0=self.psS[:].rearrange("p (a d) -> p a d", a=8),
                        in1=elocT[:, w * 8:(w + 1) * 8].unsqueeze(2).broadcast_to([128, 8, 64]),
                        op=Op.mult))
                    self.vS.inc(v.tensor_tensor(
                        out=t2b[:], in0=t1b[:],
                        in1=self.psY[:, (w % 2) * 512:(w % 2 + 1) * 512], op=Op.add))
                    if c > 0 and w == 0:
                        v.wait_ge(self.dG.h, yp_store_dG[c - 1])
                    self.vS.inc(v.tensor_tensor(
                        out=yp_c[:, w * 512:(w + 1) * 512], in0=t2b[:],
                        in1=xD[:, w * 512:(w + 1) * 512], op=Op.add))
                    self.vS.inc(v.tensor_tensor(
                        out=S_run[:, w * 512:(w + 1) * 512],
                        in0=S_run[:, w * 512:(w + 1) * 512],
                        in1=self.psM[:], op=Op.add))
                self.vS.inc(v.tensor_copy(S_run_bf[:], S_run[:]))
            v_y3_last = vwbase + 5 * 7 + 4

            @b.gpsimd
            def _(gp, c=c):
                gp.wait_ge(self.vS.h, v_y3_last)
                self.dG.inc16(gp.dma_start(
                    out=self.yp_st[c * 128:(c + 1) * 128, :], in_=yp_c[:]))
            yp_store_dG[c] = self.dG.n

            prev = {"xsd": v_xsd, "Ylast": tS_Y(7)}
            if getattr(self, "chunk_fence", False):
                self.fence()

        self.fence()
        if "dbg" in self.taps:
            dbg = self.nc.dram_tensor("tap_dbg", [128, 8192 + 4096 + 2048], F32,
                                      kind="ExternalOutput").ap()
            self.tap_t["dbg"] = dbg
            @b.gpsimd
            def _(gp):
                self.dG.inc16(gp.dma_start(out=dbg[:, 0:1024], in_=Sw[0][:]))
                self.dG.inc16(gp.dma_start(out=dbg[:, 1024:2048], in_=Sw[1][:]))
                self.dG.inc16(gp.dma_start(out=dbg[:, 2048:3072], in_=Eb[0][:]))
                self.dG.inc16(gp.dma_start(out=dbg[:, 3072:4096], in_=Eb[1][:]))
                self.dG.inc16(gp.dma_start(out=dbg[:, 4096:4224], in_=G_lt[:]))
                self.dG.inc16(gp.dma_start(out=dbg[:, 4224:4288], in_=nAcsT[:]))
                self.dG.inc16(gp.dma_start(out=dbg[:, 4288:4352], in_=elocT[:]))
                self.dG.inc16(gp.dma_start(out=dbg[:, 4352:4416], in_=decT[:]))
                self.dG.inc16(gp.dma_start(out=dbg[:, 5120:9216], in_=xs_T[:]))
                self.dG.inc16(gp.dma_start(out=dbg[:, 9216:13312], in_=xD[:]))
                self.dG.inc16(gp.dma_start(out=dbg[0:1, 4416:4480], in_=cdr[:]))
                self.dG.inc16(gp.dma_start(out=dbg[0:64, 13312:13440], in_=Acsf[:]))
            self.fence()
        self.tap_dump("yp", self.yp_st)
        self.tap_dump("xc", self.xcbh)
        self.tap_dump_sbuf("eg", self.eglobT[:])
        if "sloc" in self.tap_t:
            @b.sync
            def _(s):
                self.dS.inc16(s.dma_start(out=self.tap_t["sloc"][:, 0:IN], in_=S_run[:]))
                self.dS.inc16(s.dma_start(out=self.tap_t["sloc"][0:64, IN:IN + 1], in_=self.offc[:], allow_slow_non_contiguous=True))
        self.fence()

    def xch(self):
        b = self.block
        nc = self.nc

        @b.sync
        def _(s):
            self.dS.inc16(s.dma_start(out=self.cc_in[:, 0:IN], in_=self.S_run[:]))
            self.dS.inc16(s.dma_start(out=self.cc_in[0:64, IN:IN + 1], in_=self.offc[:],
                                      allow_slow_non_contiguous=True))
        self.fence()
        self.b_scope.close()

        sc = ExitStack()
        S = lambda *a: sc.enter_context(nc.sbuf_tensor(*a))
        gbuf = S("gbuf", [128, IN], F32)
        Sacc = S("Sacc", [128, IN], F32)
        tb = S("tbX", [128, IN], F32)
        offs4 = S("offs4", [64, 4], F32)
        wa = S("wa", [64, 2], F32)
        wcols = S("wcols", [64, 4], F32)
        wrow = S("wrow", [1, 64], F32)
        wrep = S("wrep", [128, 64], F32)

        @b.gpsimd
        def _(g):
            self.dS.wait(g)
            self.ccS.inc(g.collective_compute(
                "AllGather", Op.bypass,
                replica_groups=[[0, 1, 2, 3], [4, 5, 6, 7]],
                ins=[self.cc_in], outs=[self.cc_out]))

        @b.sync
        def _(s):
            self.ccS.wait(s)
            self.dS.inc16(s.dma_start(
                out=offs4[:],
                in_=self.cc_out.rearrange("(j p) c -> p j c", j=4)[0:64, :, IN:IN + 1].squeeze(2),
                allow_slow_non_contiguous=True))

        @b.vector
        def _(v):
            self.dS.wait(v)
            v.memset(Sacc[:], 0.0)
            for j in range(4):
                v.memset(wa[:, 0:1], 0.0)
                for m in range(4):
                    v.drain()
                    v.scalar_tensor_tensor(out=wa[:, (m + 1) % 2:(m + 1) % 2 + 1],
                                           in0=offs4[:, m:m + 1],
                                           scalar=self.selmt[:, 4 * j + m:4 * j + m + 1],
                                           in1=wa[:, m % 2:m % 2 + 1],
                                           op0=Op.mult, op1=Op.add)
                v.drain()
                v.tensor_copy(wcols[:, j:j + 1], wa[:, 0:1])
            self.vS.inc(v.drain())
        vsum = self.vS.n

        @b.scalar
        def _(s):
            s.wait_ge(self.vS.h, vsum)
            s.activation(out=wcols[:], in_=wcols[:], func=AF.Exp)
            self.sS.inc(s.drain())
        sexp = self.sS.n

        @b.vector
        def _(v):
            self.sS.wait(v)
            v.tensor_tensor(out=wcols[:], in0=wcols[:], in1=self.selt[:],
                            op=Op.mult)
            self.vS.inc(v.drain())
        vwsel = self.vS.n

        for j in range(4):
            @b.tensor
            def _(t, j=j):
                t.wait_ge(self.vS.h, vwsel if j == 0 else self.vS.n)
                self.tS.inc(t.matmul(self.psM[0:1, 0:64], wcols[:, j:j + 1],
                                     self.ident_f[0:64, 0:64], start=True, stop=True))

            @b.vector
            def _(v):
                self.tS.wait(v)
                v.tensor_copy(wrow[:], self.psM[0:1, 0:64])
                self.vS.inc(v.drain())

            @b.tensor
            def _(t):
                self.vS.wait(t)
                self.tS.inc(t.matmul(self.psM[:, 64:128], self.ones_f1[:], wrow[:],
                                     start=True, stop=True))

            @b.vector
            def _(v):
                self.tS.wait(v)
                v.tensor_copy(wrep[:], self.psM[:, 64:128])
                self.vS.inc(v.drain())

            @b.sync
            def _(s, j=j):
                if j > 0:
                    s.wait_ge(self.vS.h, self.vS.n)
                self.dS.inc16(s.dma_start(out=gbuf[:],
                                          in_=self.cc_out[j * 128:(j + 1) * 128, 0:IN]))

            @b.vector
            def _(v):
                self.dS.wait(v)
                self.vS.wait(v)
                v.tensor_tensor(out=tb[:].rearrange("p (a d) -> p a d", a=64),
                                in0=gbuf[:].rearrange("p (a d) -> p a d", a=64),
                                in1=wrep[:].unsqueeze(2).broadcast_to([128, 64, 64]),
                                op=Op.mult)
                self.vS.inc(v.tensor_tensor(out=Sacc[:], in0=Sacc[:], in1=tb[:],
                                            op=Op.add))

        @b.vector
        def _(v):
            self.vS.inc(v.tensor_copy(self.S_in_bf[:], Sacc[:]))

        if "sin" in self.tap_t:
            @b.sync
            def _(s):
                self.vS.wait(s)
                self.dS.inc16(s.dma_start(out=self.tap_t["sin"], in_=Sacc[:]))
        if "ccout" in self.taps:
            cco = self.nc.dram_tensor("tap_ccout", [4 * 128, IN + 4], F32,
                                      kind="ExternalOutput").ap()
            self.tap_t["ccout"] = cco
            @b.sync
            def _(s):
                self.ccS.wait(s)
                self.dS.inc16(s.dma_start(out=cco, in_=self.cc_out))
            cci = self.nc.dram_tensor("tap_ccin", [128, IN + 4], F32,
                                      kind="ExternalOutput").ap()
            self.tap_t["ccin"] = cci
            @b.sync
            def _(s):
                self.dS.inc16(s.dma_start(out=cci, in_=self.cc_in))
        if "xdbg" in self.taps:
            xd = self.nc.dram_tensor("tap_xdbg", [128, 16], F32,
                                     kind="ExternalOutput").ap()
            self.tap_t["xdbg"] = xd
            @b.sync
            def _(s):
                self.vS.wait(s)
                self.dS.inc16(s.dma_start(out=xd[0:64, 0:4], in_=wcols[:],
                                          allow_slow_non_contiguous=True))
                self.dS.inc16(s.dma_start(out=xd[0:64, 4:8], in_=offs4[:],
                                          allow_slow_non_contiguous=True))
                self.dS.inc16(s.dma_start(out=xd[:, 8:9], in_=wrep[:, 0:1],
                                          allow_slow_non_contiguous=True))
        self.fence()
        sc.close()

    def c1(self):
        b = self.block
        nc = self.nc
        sc = ExitStack()
        S = lambda *a: sc.enter_context(nc.sbuf_tensor(*a))
        nw = S("nw", [128, IN], F32)
        ypb = S("ypb", [128, IN], BF16)
        zb = S("zb", [128, IN], BF16)
        siluz = S("siluz", [128, IN], F32)
        yfb = S("yfb", [128, IN], F32)
        yfn = S("yfn", [128, IN], BF16)
        Cc_sb = S("Cc_sb", [128, 128], BF16)
        t1x = S("t1x", [128, 2048], F32)
        sqs = S("sqs", [128, 2048], F32)
        sqcols = S("sqcols", [128, 2], F32)
        rcol = S("rcol", [128, 1], F32)

        @b.sync
        def _(s):
            self.dS.inc16(s.dma_start(out=nw[:], in_=self.normw_rep))
        prevc = {}

        for c in range(NCH):
            dld0 = self.dS.n

            @b.sync
            def _(s, c=c):
                if c > 0:
                    s.wait_ge(self.vS.h, prevc["v_yf"])
                    s.wait_ge(self.tS.h, prevc["t_mm"])
                self.dS.inc16(s.dma_start(out=ypb[:], in_=self.yp_st[c * 128:(c + 1) * 128, :]))
                self.dS.inc16(s.dma_start(out=zb[:], in_=self.z_st[c * 128:(c + 1) * 128, :]))
                self.dS.inc16(s.dma_start(out=Cc_sb[:],
                                          in_=self.xcbh[4224:4352, c * 128:(c + 1) * 128]))
            dld = self.dS.n

            sb0 = self.sS.n

            @b.scalar
            def _(s, c=c):
                s.wait_ge(self.dS.h, dld)
                if c > 0:
                    s.wait_ge(self.vS.h, prevc["v_yf"])
                self.sS.inc(s.activation(out=siluz[:], in_=zb[:], func=AF.Silu))

            tb0 = self.tS.n
            vpred = self.vS.n

            @b.tensor
            def _(t, c=c, vpred=vpred):
                t.wait_ge(self.dS.h, dld)
                for half in range(2):
                    if half == 1:
                        t.wait_ge(self.vS.h, vpred + 1)
                    elif c > 0:
                        t.wait_ge(self.vS.h, prevc["v_yf"])
                    for j in range(32):
                        h = 32 * half + j
                        mm = t.matmul(self.psA[:, j * 64:(j + 1) * 64], Cc_sb[:],
                                      self.S_in_bf[:, h * 64:(h + 1) * 64],
                                      start=True, stop=True)
                    self.tS.inc(mm)

            @b.vector
            def _(v, c=c, sb0=sb0, tb0=tb0):
                for half in range(2):
                    v.wait_ge(self.tS.h, tb0 + half + 1)
                    egs = self.eglobT[:, c * 64 + half * 32:c * 64 + (half + 1) * 32]
                    v.tensor_tensor(out=t1x[:].rearrange("p (a d) -> p a d", a=32),
                                    in0=self.psA[:].rearrange("p (a d) -> p a d", a=32),
                                    in1=egs.unsqueeze(2).broadcast_to([128, 32, 64]),
                                    op=Op.mult)
                    v.tensor_tensor(out=t1x[:], in0=t1x[:],
                                    in1=ypb[:, half * 2048:(half + 1) * 2048], op=Op.add)
                    v.wait_ge(self.sS.h, sb0 + 1)
                    self.vS.inc(v.tensor_tensor(
                        out=yfb[:, half * 2048:(half + 1) * 2048], in0=t1x[:],
                        in1=siluz[:, half * 2048:(half + 1) * 2048], op=Op.mult))
            @b.vector
            def _(v):
                self.vS.inc(v.drain())
            v_yf = self.vS.n

            @b.scalar
            def _(s):
                s.wait_ge(self.vS.h, v_yf)
                for half in range(2):
                    s.activation(out=sqs[:], in_=yfb[:, half * 2048:(half + 1) * 2048],
                                 func=AF.Square, accum_out=sqcols[:, half:half + 1])
                self.sS.inc(s.drain())
            s_sq = self.sS.n

            @b.vector
            def _(v):
                v.wait_ge(self.sS.h, s_sq)
                v.tensor_tensor(out=rcol[:], in0=sqcols[:, 0:1],
                                in1=sqcols[:, 1:2], op=Op.add)
                self.vS.inc(v.drain())
            v_rs = self.vS.n

            @b.scalar
            def _(s):
                s.wait_ge(self.vS.h, v_rs)
                s.activation(out=rcol[:], in_=rcol[:], func=AF.Ln,
                             scale=1.0 / IN, bias=self.epscol[:])
                s.activation(out=rcol[:], in_=rcol[:], func=AF.Exp, scale=-0.5)
                self.sS.inc(s.drain())
            s_rt = self.sS.n

            @b.vector
            def _(v, c=c):
                v.wait_ge(self.sS.h, s_rt)
                for half in range(2):
                    sl = slice(half * 2048, (half + 1) * 2048)
                    stt = v.scalar_tensor_tensor(out=yfn[:, sl], in0=yfb[:, sl],
                                                 scalar=rcol[:], in1=nw[:, sl],
                                                 op0=Op.mult, op1=Op.mult)
                self.vS.inc(stt)
            v_yfn = self.vS.n

            @b.sync
            def _(s, c=c):
                s.wait_ge(self.vS.h, v_yfn)
                self.dS.inc16(s.dma_start(out=self.yfbh[c * 128:(c + 1) * 128, :],
                                          in_=yfn[:]))

            if "yfraw" in self.taps:
                if "yfraw" not in self.tap_t:
                    self.tap_t["yfraw"] = self.nc.dram_tensor(
                        "tap_yfraw", [T, IN], F32, kind="ExternalOutput").ap()
                    self.tap_t["cdbg"] = self.nc.dram_tensor(
                        "tap_cdbg", [128, 3 * NCH], F32, kind="ExternalOutput").ap()
                @b.gpsimd
                def _(g, c=c, v_yfn=v_yfn):
                    g.wait_ge(self.vS.h, v_yfn)
                    self.dG.inc16(g.dma_start(
                        out=self.tap_t["yfraw"][c * 128:(c + 1) * 128, :], in_=yfb[:]))
                    self.dG.inc16(g.dma_start(
                        out=self.tap_t["cdbg"][:, 3 * c:3 * c + 2], in_=sqcols[:],
                        allow_slow_non_contiguous=True))
                    self.dG.inc16(g.dma_start(
                        out=self.tap_t["cdbg"][:, 3 * c + 2:3 * c + 3], in_=rcol[:],
                        allow_slow_non_contiguous=True))

            prevc = {"v_yf": v_yf, "t_mm": self.tS.n}

        self.fence()
        self.tap_dump("yf", self.yfbh)
        sc.close()
        self.mid_scope.close()

    def c2(self):
        b = self.block
        nc = self.nc
        sc = ExitStack()
        S = lambda *a: sc.enter_context(nc.sbuf_tensor(*a))
        yfT = S("yfT", [128, KI * T], BF16)
        ws = [S(f"wsC{i}", [128, KI * 256], BF16) for i in range(2)]
        atst = [S(f"atst{i}", [128, T], BF16) for i in range(2)]

        dC0 = self.dC.n

        @b.scalar
        def _(s):
            for k in range(KI):
                self.dC.inc16(s.dma_start(out=yfT[:, k * T:(k + 1) * T],
                                          in_=self.yfbh[0:T, k * 128:(k + 1) * 128],
                                          transpose=True))
        dC_yft = self.dC.n

        tpred = self.tS.n

        @b.scalar
        def _(s):
            s.wait_ge(self.dG.h, self.wout_done)
            for si in range(8):
                if si >= 2:
                    s.wait_ge(self.tS.h, tpred + 8 * (si - 1))
                for k in range(KI):
                    self.dC.inc16(s.dma_start(
                        out=ws[si % 2][:, k * 256:(k + 1) * 256],
                        in_=self.bW_out[si * 256:(si + 1) * 256, k * 128:(k + 1) * 128],
                        transpose=True))

        tb0 = self.tS.n
        vb0 = self.vS.n
        dS0 = self.dS.n

        @b.tensor
        def _(t):
            for oc in range(16):
                si = oc // 2
                t.wait_ge(self.dC.h, dC_yft + 16 * KI * (si + 1))
                for tg in range(4):
                    u = oc * 4 + tg
                    if u >= 8:
                        t.wait_ge(self.vS.h, vb0 + u - 7)
                    bk = u % 8
                    for k in range(KI):
                        mm = t.matmul(self.bank(bk),
                                      ws[si % 2][:, k * 256 + (oc % 2) * 128:k * 256 + (oc % 2) * 128 + 128],
                                      yfT[:, k * T + tg * 512:k * T + (tg + 1) * 512],
                                      start=(k == 0), stop=(k == KI - 1))
                    self.tS.inc(mm)

        @b.vector
        def _(v):
            for oc in range(16):
                for tg in range(4):
                    u = oc * 4 + tg
                    v.wait_ge(self.tS.h, tb0 + u + 1)
                    if oc >= 2 and tg == 0:
                        v.wait_ge(self.dS.h, dS0 + 16 * (oc - 1))
                    self.vS.inc(v.tensor_copy(atst[oc % 2][:, tg * 512:(tg + 1) * 512],
                                              self.bank(u % 8)))

        @b.sync
        def _(s):
            for oc in range(16):
                s.wait_ge(self.vS.h, vb0 + oc * 4 + 4)
                self.dS.inc16(s.dma_start(
                    out=self.attn_st[oc * 128:(oc + 1) * 128, :], in_=atst[oc % 2][:]))

        self.fence()
        self.tap_dump("attn", self.attn_st)
        sc.close()

    def tern(self):
        b = self.block
        nc = self.nc
        sc = ExitStack()
        S = lambda *a: sc.enter_context(nc.sbuf_tensor(*a))
        wt = [S(f"wtT{i}", [128, IN], F32) for i in range(2)]
        tt = S("ttT", [128, IN], F32)
        tern = [S(f"ternT{i}", [128, IN], BF16) for i in range(2)]
        rc = S("rcT", [128, 32], F32)
        rsum = S("rsumT", [128, 1], F32)
        sval = S("svalT", [1, 1], F32)

        for which, win, wout, ntile, tw, scol, rscol in [
                ("up", self.w_up, self.tW_up, 32, H, self.supc, self.rsup),
                ("down", self.w_down, self.tW_down, 16, IN, self.sdnc, self.rsdn)]:
            nelem = float(ntile * 128 * tw)
            dG0 = self.dG.n
            vpred = self.vS.n

            @b.gpsimd
            def _(g, win=win, ntile=ntile, tw=tw, vpred=vpred):
                for i in range(ntile):
                    if i >= 2:
                        g.wait_ge(self.vS.h, vpred + i - 1)
                    self.dG.inc16(g.dma_start(out=wt[i % 2][:, 0:tw],
                                              in_=win[i * 128:(i + 1) * 128, :]))
            vb0 = self.vS.n

            @b.vector
            def _(v, ntile=ntile, tw=tw, dG0=dG0):
                for i in range(ntile):
                    v.wait_ge(self.dG.h, dG0 + 16 * (i + 1))
                    self.vS.inc(v.tensor_reduce(out=rc[:, i:i + 1], in_=wt[i % 2][:, 0:tw],
                                                axis=AX, op=Op.add,
                                                apply_absolute_value=True))
                v.drain()
                v.tensor_reduce(out=rsum[:], in_=rc[:, 0:ntile], axis=AX, op=Op.add)
                self.vS.inc(v.drain())

            @b.tensor
            def _(t):
                self.vS.wait(t)
                self.tS.inc(t.matmul(self.psM[0:1, 0:1], rsum[:], self.onecol_f[:],
                                     start=True, stop=True))

            @b.vector
            def _(v):
                self.tS.wait(v)
                v.tensor_copy(sval[:], self.psM[0:1, 0:1])
                self.vS.inc(v.drain())

            @b.tensor
            def _(t):
                self.vS.wait(t)
                self.tS.inc(t.matmul(self.psM[:, 64:65], self.ones_f1[:], sval[:],
                                     start=True, stop=True))

            @b.vector
            def _(v, nelem=nelem, scol=scol, rscol=rscol):
                self.tS.wait(v)
                v.tensor_copy(scol[:], self.psM[:, 64:65])
                v.drain()
                v.tensor_scalar(out=scol[:], in0=scol[:],
                                scalar1=1.0 / nelem, scalar2=1e-5,
                                op0=Op.mult, op1=Op.max)
                self.vS.inc(v.drain())
            vsc = self.vS.n

            @b.scalar
            def _(s, scol=scol, rscol=rscol, vsc=vsc):
                s.wait_ge(self.vS.h, vsc)
                s.activation(out=rscol[:], in_=scol[:], func=AF.Ln)
                s.activation(out=rscol[:], in_=rscol[:], func=AF.Exp, scale=-1.0)
                self.sS.inc(s.drain())
            s_rs = self.sS.n

            @b.vector
            def _(v, s_rs=s_rs):
                v.wait_ge(self.sS.h, s_rs)
                self.vS.inc(v.drain())
            v_s = self.vS.n

            dG1 = self.dG.n

            @b.gpsimd
            def _(g, win=win, wout=wout, ntile=ntile, tw=tw, v_s=v_s, vb0=vb0):
                for i in range(ntile):
                    g.wait_ge(self.vS.h, v_s + max(0, (i - 1)) * 3)
                    self.dG.inc16(g.dma_start(out=wt[i % 2][:, 0:tw],
                                              in_=win[i * 128:(i + 1) * 128, :]))

            dS_t0 = self.dS.n

            @b.vector
            def _(v, ntile=ntile, tw=tw, dG1=dG1, rscol=rscol, v_s=v_s, dS_t0=dS_t0):
                for i in range(ntile):
                    v.wait_ge(self.dG.h, dG1 + 16 * (i + 1))
                    if i >= 2:
                        v.wait_ge(self.dS.h, dS_t0 + 16 * (i - 1))
                    v.tensor_scalar(out=tt[:, 0:tw], in0=wt[i % 2][:, 0:tw],
                                    scalar1=rscol[:], scalar2=MAGIC,
                                    op0=Op.mult, op1=Op.add)
                    v.tensor_scalar(out=tt[:, 0:tw], in0=tt[:, 0:tw],
                                    scalar1=MAGIC, scalar2=1.0,
                                    op0=Op.subtract, op1=Op.min)
                    self.vS.inc(v.tensor_scalar(out=tern[i % 2][:, 0:tw],
                                                in0=tt[:, 0:tw], scalar1=-1.0,
                                                scalar2=None, op0=Op.max), 3)

            @b.sync
            def _(s, wout=wout, ntile=ntile, tw=tw, v_s=v_s):
                for i in range(ntile):
                    s.wait_ge(self.vS.h, v_s + (i + 1) * 3)
                    self.dS.inc16(s.dma_start(out=wout[i * 128:(i + 1) * 128, :],
                                              in_=tern[i % 2][:, 0:tw]))
        self.fence()
        self.tap_dump("tup", self.tW_up)
        self.tap_dump("tdn", self.tW_down)
        sc.close()

    def _ln_apply(self, c, xsrc, xout, scr, mcol, sqcol, vtag, dbg=None):
        b = self.block

        @b.vector
        def _(v):
            v.tensor_reduce(out=mcol[:], in_=xsrc[:], axis=AX, op=Op.add)
            v.tensor_tensor(out=scr[:], in0=xsrc[:], in1=xsrc[:], op=Op.mult)
            v.tensor_reduce(out=sqcol[:], in_=scr[:], axis=AX, op=Op.add)
            v.tensor_scalar(out=mcol[:], in0=mcol[:], scalar1=-1.0 / H,
                            scalar2=None, op0=Op.mult)
            v.tensor_scalar(out=sqcol[:], in0=sqcol[:], scalar1=1.0 / H,
                            scalar2=None, op0=Op.mult)
            v.tensor_tensor(out=scr[:, 0:1], in0=mcol[:], in1=mcol[:], op=Op.mult)
            v.drain()
            v.tensor_tensor(out=sqcol[:], in0=sqcol[:], in1=scr[:, 0:1],
                            op=Op.subtract)
            self.vS.inc(v.drain())
        v_v = self.vS.n

        @b.scalar
        def _(s):
            s.wait_ge(self.vS.h, v_v)
            s.activation(out=sqcol[:], in_=sqcol[:], func=AF.Ln,
                         bias=self.epscol[:])
            s.activation(out=sqcol[:], in_=sqcol[:], func=AF.Exp, scale=-0.5)
            self.sS.inc(s.drain())
        s_r = self.sS.n

        @b.vector
        def _(v):
            v.wait_ge(self.sS.h, s_r)
            v.tensor_scalar(out=xout[:], in0=xsrc[:], scalar1=mcol[:],
                            scalar2=sqcol[:], op0=Op.add, op1=Op.mult)
            self.vS.inc(v.drain())
        return self.vS.n

    def dup(self):
        b = self.block
        nc = self.nc
        sc = ExitStack()
        S = lambda *a: sc.enter_context(nc.sbuf_tensor(*a))
        wupT = S("wupT", [128, KH * IN], BF16)
        atT = S("atT", [128, H], BF16)
        xch = S("xch", [128, H], F32)
        x1r = S("x1r", [128, H], F32)
        x1 = S("x1", [128, H], F32)
        q1 = S("q1", [128, H], BF16)
        q1T = S("q1T", [128, H], BF16)
        hb = S("hb", [128, IN], F32)
        q2 = x1r[:].bitcast(BF16)
        q2T = None
        scr = S("scrD", [128, H], F32)
        mcol = S("mcolD", [128, 1], F32)
        sqcol = S("sqcolD", [128, 1], F32)
        amax = S("amaxD", [128, 1], F32)
        scl = S("sclD", [128, 1], F32)
        c1col = S("c1colD", [128, 1], F32)
        lndbg = S("lndbg", [128, 4 + 2048], F32)
        psA_bf = self.psA[:].bitcast(BF16)
        q2T = hb[:].bitcast(BF16)[:, 0:IN]

        @b.scalar
        def _(s):
            self.gS.wait(s)
            for k in range(KH):
                self.dC.inc16(s.dma_start(out=wupT[:, k * IN:(k + 1) * IN],
                                          in_=self.tW_up[0:IN, k * 128:(k + 1) * 128],
                                          transpose=True))
        dC_w = self.dC.n
        prevd = {}

        for c in range(NCH):
            @b.sync
            def _(s, c=c):
                if c > 0:
                    s.wait_ge(self.vS.h, prevd["v_x1"])
                    s.wait_ge(self.tS.h, prevd["t_q2"])
                self.dS.inc16(s.dma_start(out=xch[:],
                                          in_=self.xh[3 + c * 128:3 + (c + 1) * 128, :]))
            dxc = self.dS.n

            @b.scalar
            def _(s, c=c):
                if c > 0:
                    s.wait_ge(self.vS.h, prevd["v_x1"])
                self.dC.inc16(s.dma_start(out=atT[:],
                                          in_=self.attn_st[0:H, c * 128:(c + 1) * 128],
                                          transpose=True))
            datt = self.dC.n

            @b.vector
            def _(v, c=c):
                v.wait_ge(self.dS.h, dxc)
                v.wait_ge(self.dC.h, datt)
                if c > 0:
                    v.wait_ge(self.tS.h, prevd["t_q2"])
                self.vS.inc(v.tensor_tensor(out=x1r[:], in0=xch[:], in1=atT[:],
                                            op=Op.add))

            v_x1 = self._ln_apply(c, x1r, x1, scr, mcol, sqcol, "ln1",
                                  dbg=(lndbg if c == 0 else None))

            if "x1d" in self.taps:
                if "x1d" not in self.tap_t:
                    self.tap_t["x1d"] = self.nc.dram_tensor(
                        "tap_x1d", [T, H], F32, kind="ExternalOutput").ap()
                    self.tap_t["ddbg"] = self.nc.dram_tensor(
                        "tap_ddbg", [128, NCH * 4], F32, kind="ExternalOutput").ap()
                    self.tap_t["sqd"] = self.nc.dram_tensor(
                        "tap_sqd", [128, 2 * H], F32, kind="ExternalOutput").ap()
                if c == 0:
                    @b.gpsimd
                    def _(g, v_x1=v_x1):
                        g.wait_ge(self.vS.h, v_x1)
                        self.dG.inc16(g.dma_start(out=self.tap_t["sqd"][:, 0:H],
                                                  in_=lndbg[:, 4:2052]))
                @b.gpsimd
                def _(g, c=c, v_x1=v_x1):
                    g.wait_ge(self.vS.h, v_x1)
                    self.dG.inc16(g.dma_start(
                        out=self.tap_t["x1d"][c * 128:(c + 1) * 128, :], in_=x1[:]))
                    self.dG.inc16(g.dma_start(
                        out=self.tap_t["ddbg"][:, 4 * c:4 * c + 1], in_=mcol[:],
                        allow_slow_non_contiguous=True))
                    self.dG.inc16(g.dma_start(
                        out=self.tap_t["ddbg"][:, 4 * c + 1:4 * c + 2], in_=sqcol[:],
                        allow_slow_non_contiguous=True))
                    self.dG.inc16(g.dma_start(
                        out=self.tap_t["ddbg"][:, 4 * c + 2:4 * c + 3], in_=x1r[:, 0:1],
                        allow_slow_non_contiguous=True))
                    if c == 0:
                        self.dG.inc16(g.dma_start(
                            out=self.tap_t["ddbg"][:, 60:64], in_=lndbg[:, 0:4]))

            @b.vector
            def _(v, c=c):
                v.tensor_reduce(out=amax[:], in_=x1[:], axis=AX, op=Op.max,
                                apply_absolute_value=True)
                v.drain()
                v.tensor_scalar(out=amax[:], in0=amax[:], scalar1=1e-5,
                                scalar2=None, op0=Op.max)
                v.drain()
                v.tensor_tensor(out=c1col[:], in0=amax[:], in1=self.supc[:],
                                op=Op.mult)
                v.drain()
                v.tensor_scalar(out=c1col[:], in0=c1col[:], scalar1=1.0 / 127.0,
                                scalar2=None, op0=Op.mult)
                self.vS.inc(v.drain())
            v_am1 = self.vS.n

            @b.scalar
            def _(s):
                s.wait_ge(self.vS.h, v_am1)
                s.activation(out=scl[:], in_=amax[:], func=AF.Ln)
                s.activation(out=scl[:], in_=scl[:], func=AF.Exp,
                             scale=-1.0, bias=self.ln127c[:])
                self.sS.inc(s.drain())
            s_sc1 = self.sS.n

            @b.vector
            def _(v):
                v.wait_ge(self.sS.h, s_sc1)
                v.tensor_scalar(out=scr[:], in0=x1[:], scalar1=scl[:],
                                scalar2=MAGIC, op0=Op.mult, op1=Op.add)
                v.tensor_scalar(out=scr[:], in0=scr[:], scalar1=MAGIC,
                                scalar2=127.0, op0=Op.subtract, op1=Op.min)
                v.tensor_scalar(out=q1[:], in0=scr[:], scalar1=-127.0,
                                scalar2=None, op0=Op.max)
                self.vS.inc(v.drain())
            v_q1 = self.vS.n

            tq0 = self.tS.n

            @b.tensor
            def _(t, c=c):
                t.wait_ge(self.vS.h, v_q1)
                if c > 0:
                    t.wait_ge(self.sS.h, prevd["s_noq"])
                for k in range(KH):
                    mm = t.transpose(psA_bf[:, k * 128:(k + 1) * 128],
                                     q1[:, k * 128:(k + 1) * 128], self.ident[:])
                self.tS.inc(mm)

            @b.vector
            def _(v):
                v.wait_ge(self.tS.h, tq0 + 1)
                v.tensor_copy(q1T[:], psA_bf[:, 0:H])
                self.vS.inc(v.drain())
            v_q1t = self.vS.n

            tu0 = self.tS.n
            vu0 = self.vS.n

            @b.tensor
            def _(t, c=c):
                t.wait_ge(self.vS.h, v_q1t)
                t.wait_ge(self.dC.h, dC_w)
                for g in range(8):
                    bk = 2 + (g % 6)
                    if g >= 6:
                        t.wait_ge(self.vS.h, vu0 + g - 5)
                    for k in range(KH):
                        mm = t.matmul(self.bank(bk),
                                      q1T[:, k * 128:(k + 1) * 128],
                                      wupT[:, k * IN + g * 512:k * IN + (g + 1) * 512],
                                      start=(k == 0), stop=(k == KH - 1))
                    self.tS.inc(mm)

            @b.vector
            def _(v, c=c):
                if c > 0:
                    v.wait_ge(self.dS.h, prevd["d_q2st"])
                for g in range(8):
                    bk = 2 + (g % 6)
                    v.wait_ge(self.tS.h, tu0 + g + 1)
                    self.vS.inc(v.tensor_scalar(out=hb[:, g * 512:(g + 1) * 512],
                                                in0=self.bank(bk), scalar1=0.0,
                                                scalar2=c1col[:], op0=Op.max,
                                                op1=Op.mult))
            @b.vector
            def _(v):
                self.vS.inc(v.drain())
            v_h = self.vS.n

            @b.scalar
            def _(s):
                s.wait_ge(self.vS.h, v_h)
                s.activation(out=hb[:], in_=hb[:], func=AF.Square)
                self.sS.inc(s.drain())
            s_noq = self.sS.n

            @b.vector
            def _(v, c=c):
                v.wait_ge(self.sS.h, s_noq)
                v.tensor_reduce(out=amax[:], in_=hb[:], axis=AX, op=Op.max,
                                apply_absolute_value=True)
                v.drain()
                v.tensor_scalar(out=amax[:], in0=amax[:], scalar1=1e-5,
                                scalar2=None, op0=Op.max)
                v.drain()
                v.tensor_tensor(out=self.inv2c[:, c:c + 1], in0=amax[:],
                                in1=self.sdnc[:], op=Op.mult)
                v.drain()
                v.tensor_scalar(out=self.inv2c[:, c:c + 1],
                                in0=self.inv2c[:, c:c + 1],
                                scalar1=1.0 / 127.0, scalar2=None, op0=Op.mult)
                self.vS.inc(v.drain())
            v_am2 = self.vS.n

            @b.scalar
            def _(s):
                s.wait_ge(self.vS.h, v_am2)
                s.activation(out=scl[:], in_=amax[:], func=AF.Ln)
                s.activation(out=scl[:], in_=scl[:], func=AF.Exp,
                             scale=-1.0, bias=self.ln127c[:])
                self.sS.inc(s.drain())
            s_sc2 = self.sS.n

            @b.vector
            def _(v):
                v.wait_ge(self.sS.h, s_sc2)
                v.tensor_scalar(out=hb[:], in0=hb[:], scalar1=scl[:],
                                scalar2=MAGIC, op0=Op.mult, op1=Op.add)
                v.tensor_scalar(out=hb[:], in0=hb[:], scalar1=MAGIC,
                                scalar2=127.0, op0=Op.subtract, op1=Op.min)
                v.tensor_scalar(out=q2, in0=hb[:], scalar1=-127.0,
                                scalar2=None, op0=Op.max)
                self.vS.inc(v.drain())
            v_q2 = self.vS.n

            tq2 = self.tS.n

            @b.tensor
            def _(t):
                t.wait_ge(self.vS.h, v_q2)
                for k in range(KI):
                    mm = t.transpose(psA_bf[:, k * 128:(k + 1) * 128],
                                     q2[:, k * 128:(k + 1) * 128], self.ident[:])
                self.tS.inc(mm)
            t_q2 = self.tS.n

            @b.vector
            def _(v):
                v.wait_ge(self.tS.h, tq2 + 1)
                v.tensor_copy(q2T, psA_bf[:, 0:IN])
                self.vS.inc(v.drain())
            v_q2t = self.vS.n

            @b.sync
            def _(s, c=c):
                s.wait_ge(self.vS.h, v_q2t)
                self.dS.inc16(s.dma_start(out=self.q2bh[c * 128:(c + 1) * 128, :],
                                          in_=q2T))
            d_q2st = self.dS.n

            prevd = {"v_x1": v_x1, "s_noq": s_noq, "t_q2": t_q2,
                     "d_q2st": d_q2st}

        self.fence()
        self.tap_dump("q2", self.q2bh)
        if "x1" in self.tap_t:
            pass
        sc.close()

    def ddn(self):
        b = self.block
        nc = self.nc
        sc = ExitStack()
        S = lambda *a: sc.enter_context(nc.sbuf_tensor(*a))
        wdnT = S("wdnT", [128, KI * H], BF16)
        atT = S("atT2", [128, H], BF16)
        xch = S("xch2", [128, H], F32)
        x1r = S("x1r2", [128, H], F32)
        x1 = S("x12", [128, H], F32)
        q2T = S("q2T2", [128, IN], BF16)
        x2 = S("x2o", [128, H], F32)
        outf = S("outf", [128, H], F32)
        scr = S("scrN", [128, H], F32)
        mcol = S("mcolN", [128, 1], F32)
        sqcol = S("sqcolN", [128, 1], F32)

        @b.scalar
        def _(s):
            for k in range(KI):
                self.dC.inc16(s.dma_start(out=wdnT[:, k * H:(k + 1) * H],
                                          in_=self.tW_down[0:H, k * 128:(k + 1) * 128],
                                          transpose=True))
        dC_w = self.dC.n
        prevn = {}

        for c in range(NCH):
            @b.sync
            def _(s, c=c):
                if c > 0:
                    s.wait_ge(self.vS.h, prevn["v_out"])
                    s.wait_ge(self.tS.h, prevn["t_mm"])
                self.dS.inc16(s.dma_start(out=xch[:],
                                          in_=self.xh[3 + c * 128:3 + (c + 1) * 128, :]))
                self.dS.inc16(s.dma_start(out=q2T[:],
                                          in_=self.q2bh[c * 128:(c + 1) * 128, :]))
            dxc = self.dS.n

            @b.scalar
            def _(s, c=c):
                if c > 0:
                    s.wait_ge(self.vS.h, prevn["v_out"])
                self.dC.inc16(s.dma_start(out=atT[:],
                                          in_=self.attn_st[0:H, c * 128:(c + 1) * 128],
                                          transpose=True))
            datt = self.dC.n

            @b.vector
            def _(v):
                v.wait_ge(self.dS.h, dxc)
                v.wait_ge(self.dC.h, datt)
                self.vS.inc(v.tensor_tensor(out=x1r[:], in0=xch[:], in1=atT[:],
                                            op=Op.add))
            v_x1 = self._ln_apply(c, x1r, x1, scr, mcol, sqcol, "ln1b")

            td0 = self.tS.n
            vd0 = self.vS.n

            @b.tensor
            def _(t, c=c):
                t.wait_ge(self.dS.h, dxc)
                t.wait_ge(self.dC.h, dC_w)
                if c > 0:
                    t.wait_ge(self.vS.h, prevn["v_ev"])
                for g in range(4):
                    for k in range(KI):
                        mm = t.matmul(self.bank(g),
                                      q2T[:, k * 128:(k + 1) * 128],
                                      wdnT[:, k * H + g * 512:k * H + (g + 1) * 512],
                                      start=(k == 0), stop=(k == KI - 1))
                    self.tS.inc(mm)

            @b.vector
            def _(v, c=c, v_x1=v_x1, td0=td0):
                v.wait_ge(self.vS.h, v_x1)
                for g in range(4):
                    v.wait_ge(self.tS.h, td0 + g + 1)
                    self.vS.inc(v.scalar_tensor_tensor(
                        out=x2[:, g * 512:(g + 1) * 512], in0=self.bank(g),
                        scalar=self.inv2c[:, c:c + 1],
                        in1=x1[:, g * 512:(g + 1) * 512],
                        op0=Op.mult, op1=Op.add))
            @b.vector
            def _(v):
                self.vS.inc(v.drain())
            v_ev = self.vS.n

            v_out = self._ln_apply(c, x2, outf, scr, mcol, sqcol, "ln2")

            @b.sync
            def _(s, c=c, v_out=v_out):
                s.wait_ge(self.vS.h, v_out)
                self.dS.inc16(s.dma_start(out=self.out[c * 128:(c + 1) * 128, :],
                                          in_=outf[:]))
            if "x1" in self.tap_t:
                @b.sync
                def _(s, c=c, v_x1=v_x1):
                    s.wait_ge(self.vS.h, v_x1)
                    self.dS.inc16(s.dma_start(
                        out=self.tap_t["x1"][c * 128:(c + 1) * 128, :], in_=x1[:]))

            prevn = {"v_out": v_out, "t_mm": self.tS.n, "v_ev": v_ev}

        self.out_written = True
        self.fence()
        sc.close()

    def build(self, upto="full"):
        self.pro()
        if upto == "pro":
            self.finish(); return self._pack()
        self.a1()
        if upto == "a1":
            self.finish(); return self._pack()
        self.a2()
        if upto == "b1":
            self.b_phase(nchunks=1)
            self.finish(); return self._pack()
        if upto == "b2":
            self.b_phase(nchunks=2)
            self.finish(); return self._pack()
        if upto == "b":
            self.b_phase()
            self.finish(); return self._pack()
        self.b_phase()
        self.xch()
        if upto == "xch":
            self.finish(); return self._pack()
        self.c1()
        if upto == "c1":
            self.finish(); return self._pack()
        self.c2()
        if upto == "c2":
            self.finish(); return self._pack()
        self.tern()
        if upto == "tern":
            self.finish(); return self._pack()
        self.dup()
        if upto == "dup":
            self.finish(); return self._pack()
        self.ddn()
        self.finish()
        return self._pack()

    def _pack(self):
        return self.nc, self.tap_t

    def finish(self):
        b = self.block
        done = "out_written" in self.__dict__
        if not done:
            with self.nc.sbuf_tensor("zout", [128, H], F32) as zo:
                @b.vector
                def _(v):
                    self.vS.inc(v.memset(zo[:], 0.0))

                @b.sync
                def _(s):
                    self.vS.wait(s)
                    for c in range(NCH):
                        self.dS.inc16(s.dma_start(out=self.out[c * 128:(c + 1) * 128, :],
                                                  in_=zo[:]))
        self.fence()


_CACHE = {}


def get_program(taps=(), upto="full"):
    key = (tuple(sorted(taps)), upto)
    if key not in _CACHE:
        bld = Builder(taps)
        _CACHE[key] = bld.build(upto)
    return _CACHE[key]


def make_in_maps(inputs):
    hs = np.asarray(inputs["L0_attn_hidden_states"])
    conv_w = np.asarray(inputs["L0_attn_conv_w"]).reshape(4352, 4)
    cwc = np.ascontiguousarray(
        conv_w.reshape(34, 128, 4).transpose(1, 0, 2).reshape(128, 136))
    cbc = np.ascontiguousarray(np.asarray(inputs["L0_attn_conv_b"]).reshape(34, 128).T)
    nwr = np.ascontiguousarray(
        np.broadcast_to(np.asarray(inputs["L0_attn_norm_w"])[None, :], (128, IN)))
    drep = np.ascontiguousarray(
        np.broadcast_to(np.asarray(inputs["L0_attn_D"])[None, :], (128, 64)))
    maps = []
    for core in range(8):
        bb, qq = core // 4, core % 4
        x_tok = hs[bb, qq * T:(qq + 1) * T]
        halo = np.zeros((3, H), np.float32) if qq == 0 else hs[bb, qq * T - 3:qq * T]
        sel = np.zeros((64, 4), np.float32)
        selm = np.zeros((64, 16), np.float32)
        for j in range(4):
            if j < qq:
                sel[:, j] = 1.0
                for m in range(j + 1, qq):
                    selm[:, 4 * j + m] = 1.0
        maps.append({
            "xh": np.ascontiguousarray(np.concatenate([halo, x_tok], 0)),
            "w_in": np.asarray(inputs["L0_attn_W_in"]),
            "w_out": np.asarray(inputs["L0_attn_W_out"]),
            "w_up": np.asarray(inputs["L0_ffn_w_up"]),
            "w_down": np.asarray(inputs["L0_ffn_w_down"]),
            "cwcols": cwc, "cbcols": cbc,
            "dtb_col": np.ascontiguousarray(np.asarray(inputs["L0_attn_dt_bias"]).reshape(64, 1)),
            "alog_col": np.ascontiguousarray(np.asarray(inputs["L0_attn_A_log"]).reshape(64, 1)),
            "d_rep": drep, "normw_rep": nwr,
            "selm_rep": selm, "sel_rep": sel,
        })
    return maps


def kernel(**inputs):
    nc, tap_t = get_program()
    maps = make_in_maps(inputs)
    res = run_bass_kernel_spmd(nc, maps, core_ids=list(range(8)))
    out = np.empty((2, 8192, H), dtype=np.float32)
    for core in range(8):
        bb, qq = core // 4, core % 4
        out[bb, qq * T:(qq + 1) * T] = res.results[core]["out"]
    return out
